# version 01 (final)
# baseline (speedup 1.0000x reference)
import contextlib
import numpy as np
import concourse.bass as bass
import concourse.mybir as mybir
from concourse.bass_utils import run_bass_kernel_spmd

F32 = mybir.dt.float32
BF16 = mybir.dt.bfloat16
I16 = mybir.dt.int16
AF = mybir.ActivationFunctionType
ALU = mybir.AluOpType
AXX = mybir.AxisListType.X

NCORES = 8
D = 1024
T = 2048
NCH = 8
HL = 512
FF = 4096
NFC = 32
PI = float(np.pi)
TWO_PI = float(2.0 * np.pi)
ATT_SCALE = float(np.float32(128.0) ** np.float32(-0.5))
EPSQ = 1e-5
EPSLN = 1e-5
SLAB = 256
NSLAB = T // SLAB

ENGINES = ("sync", "scalar", "vector", "tensor", "gpsimd")


class Sem:
    __slots__ = ("h", "n")

    def __init__(self, h):
        self.h = h
        self.n = 0

    def inc(self, k=1):
        self.n += k
        return self.n


class KB:
    def __init__(self, nc):
        self.nc = nc
        self.q = {e: [] for e in ENGINES}
        self.ctx = contextlib.ExitStack()

    def sem(self, name):
        return Sem(self.ctx.enter_context(self.nc.semaphore(name)))

    def on(self, eng, fn):
        self.q[eng].append(fn)

    def wait(self, eng, sem, val):
        if val > 0:
            self.q[eng].append(lambda e, s=sem, v=val: e.wait_ge(s.h, v))

    def finalize(self):
        nc = self.nc
        with nc.Block() as block:
            @block.sync
            def _(e):
                for f in self.q["sync"]:
                    f(e)

            @block.scalar
            def _(e):
                for f in self.q["scalar"]:
                    f(e)

            @block.vector
            def _(e):
                for f in self.q["vector"]:
                    f(e)

            @block.tensor
            def _(e):
                for f in self.q["tensor"]:
                    f(e)

            @block.gpsimd
            def _(e):
                for f in self.q["gpsimd"]:
                    f(e)
        self.ctx.close()


def host_layouts(inp):
    g = {}
    x = np.asarray(inp["L0_attn_x"], np.float32)
    g["_x_cores"] = [np.ascontiguousarray(x[c].T) for c in range(NCORES)]
    g["BreT"] = np.ascontiguousarray(np.asarray(inp["L0_attn_B_re"], np.float32).T)
    g["BimT"] = np.ascontiguousarray(np.asarray(inp["L0_attn_B_im"], np.float32).T)
    g["CreT"] = np.ascontiguousarray(np.asarray(inp["L0_attn_C_re"], np.float32).T)
    g["CimT"] = np.ascontiguousarray(np.asarray(inp["L0_attn_C_im"], np.float32).T)
    g["th"] = np.ascontiguousarray(np.asarray(inp["L0_attn_theta_log"], np.float32).reshape(4, 128).T)
    g["nu"] = np.ascontiguousarray(np.asarray(inp["L0_attn_nu_log"], np.float32).reshape(4, 128).T)
    g["ga"] = np.ascontiguousarray(np.asarray(inp["L0_attn_gamma_log"], np.float32).reshape(4, 128).T)
    g["Dv"] = np.ascontiguousarray(np.asarray(inp["L0_attn_D"], np.float32).reshape(8, 128).T)
    g["W1T"] = np.ascontiguousarray(np.asarray(inp["L0_ffn_W1"], np.float32).T)
    g["W2T"] = np.ascontiguousarray(np.asarray(inp["L0_ffn_W2"], np.float32).T)
    g["fb1"] = np.ascontiguousarray(np.asarray(inp["L0_ffn_b1"], np.float32).reshape(32, 128).T)
    g["fb2"] = np.ascontiguousarray(np.asarray(inp["L0_ffn_b2"], np.float32).reshape(8, 128).T)
    g["wq"] = np.ascontiguousarray(np.asarray(inp["L1_attn_wq"], np.float32))
    g["wkdT"] = np.ascontiguousarray(np.asarray(inp["L1_attn_wk_down"], np.float32).T)
    g["wvdT"] = np.ascontiguousarray(np.asarray(inp["L1_attn_wv_down"], np.float32).T)
    g["wku"] = np.ascontiguousarray(np.asarray(inp["L1_attn_wk_up"], np.float32))
    g["wvu"] = np.ascontiguousarray(np.asarray(inp["L1_attn_wv_up"], np.float32))
    g["wo"] = np.ascontiguousarray(np.asarray(inp["L1_attn_wo"], np.float32))
    g["rwT"] = np.ascontiguousarray(np.asarray(inp["L1_ffn_rw"], np.float32).T)
    g["rb"] = np.ascontiguousarray(np.asarray(inp["L1_ffn_rb"], np.float32).reshape(8, 1))
    g["ew1"] = np.ascontiguousarray(np.asarray(inp["L1_ffn_w1"], np.float32))
    g["ew2"] = np.ascontiguousarray(np.asarray(inp["L1_ffn_w2"], np.float32))
    g["eb1"] = np.ascontiguousarray(
        np.asarray(inp["L1_ffn_b1"], np.float32).reshape(8, 32, 128).transpose(0, 2, 1))
    g["eb2"] = np.ascontiguousarray(
        np.asarray(inp["L1_ffn_b2"], np.float32).reshape(8, 8, 128).transpose(0, 2, 1))
    g["sw1"] = np.ascontiguousarray(np.asarray(inp["L1_ffn_sw1"], np.float32))
    g["sw2"] = np.ascontiguousarray(np.asarray(inp["L1_ffn_sw2"], np.float32))
    g["sb1"] = np.ascontiguousarray(np.asarray(inp["L1_ffn_sb1"], np.float32).reshape(32, 128).T)
    g["sb2"] = np.ascontiguousarray(np.asarray(inp["L1_ffn_sb2"], np.float32).reshape(8, 128).T)
    g["B2R"] = np.ascontiguousarray(np.concatenate(
        [np.asarray(inp["L1_ffn_b2"], np.float32).reshape(8, 1024),
         np.asarray(inp["L1_ffn_sb2"], np.float32).reshape(1, 1024)], axis=0))
    return g


def host_consts():
    c = {}
    p = np.arange(128)
    e_of = p // 16
    i_of = p % 16
    c["G8"] = (e_of[:, None] == e_of[None, :]).astype(np.float32)
    c["MLT"] = ((e_of[:, None] == e_of[None, :]) & (i_of[:, None] < i_of[None, :])).astype(np.float32)
    c["SUME"] = (i_of[:, None] == i_of[None, :]).astype(np.float32)
    for d_ in range(1, 8):
        c["ROT%d" % d_] = ((i_of[:, None] == i_of[None, :]) &
                           (e_of[:, None] == ((e_of[None, :] + d_) % 8))).astype(np.float32)
    c["B8"] = (np.arange(8)[:, None] == e_of[None, :]).astype(np.float32)
    c["ONES128"] = np.ones((128, 1), np.float32)
    c["ONESROW"] = np.ones((1, 128), np.float32)
    sel = np.zeros((8, 16 * 128), np.float32)
    for i in range(16):
        for e in range(8):
            sel[e, i * 128 + e * 16 + i] = 1.0
    c["SEL"] = sel
    sela = np.zeros((128, 16 * 9), np.float32)
    selb = np.zeros((128, 16 * 9), np.float32)
    for i in range(16):
        for e in range(8):
            sela[e * 16 + i, i * 9 + e] = 1.0
        selb[i, i * 9 + 8] = 1.0
    c["SELA"] = sela
    c["SELB"] = selb
    bsel = np.zeros((9, 9 * 128), np.float32)
    for e in range(9):
        bsel[e, e * 128:(e + 1) * 128] = 1.0
    c["BSEL"] = bsel
    return c


def fm3(ap2d, p=128):
    return ap2d.rearrange("(c p) f -> p c f", p=p)


def sb3(sb_ap, c):
    f = sb_ap.shape[1] // c
    return sb_ap.rearrange("p (c f) -> p c f", c=c)


def build_program(debug=()):
    dbg = set(debug)
    nc = bass.Bass("TRN2", target_bir_lowering=False)
    dr = {}

    def din(name, shape, dt=F32):
        dr[name] = nc.dram_tensor(name, list(shape), dt, kind="ExternalInput")

    def dint(name, shape, dt=F32):
        dr[name] = nc.dram_tensor(name, list(shape), dt)

    din("x_fm", (D, T))
    for n, shape in [("BreT", (D, HL)), ("BimT", (D, HL)), ("CreT", (HL, D)), ("CimT", (HL, D)),
                     ("th", (128, 4)), ("nu", (128, 4)), ("ga", (128, 4)), ("Dv", (128, 8)),
                     ("W1T", (D, FF)), ("W2T", (FF, D)), ("fb1", (128, 32)), ("fb2", (128, 8)),
                     ("wq", (D, D)), ("wkdT", (64, D)), ("wvdT", (64, D)),
                     ("wku", (32, 128)), ("wvu", (32, 128)), ("wo", (D, D)),
                     ("rwT", (D, 8)), ("rb", (8, 1)),
                     ("ew1", (8, D, FF)), ("ew2", (8, FF, D)),
                     ("eb1", (8, 128, 32)), ("eb2", (8, 128, 8)),
                     ("sw1", (D, FF)), ("sw2", (FF, D)), ("sb1", (128, 32)), ("sb2", (128, 8)), ("B2R", (9, 1024)),
                     ("G8", (128, 128)), ("MLT", (128, 128)), ("SUME", (128, 128)),
                     ("B8", (8, 128)), ("ONES128", (128, 1)), ("ONESROW", (1, 128)),
                     ("RMASK", (8, 8)), ("SEL", (8, 16 * 128)),
                     ("SELA", (128, 16 * 9)), ("SELB", (128, 16 * 9)), ("BSEL", (9, 9 * 128))]:
        din(n, shape)
    for d_ in range(1, 8):
        din("ROT%d" % d_, (128, 128))

    dr["y_out"] = nc.dram_tensor("y_out", [D, T], F32, kind="ExternalOutput")
    dint("x1_spill", (D, T))
    dint("W1q_h", (D, FF), BF16)
    dint("W2q_h", (FF, D), BF16)
    dint("ag_in", (1, 8))
    dint("ag_out", (NCORES, 8))
    dint("ag2_in", (1, 8))
    dint("ag2_out", (NCORES, 8))

    dbg_h = {}

    def dbg_out(name, shape, dt=F32):
        if name in dbg:
            if name not in dbg_h:
                dbg_h[name] = nc.dram_tensor("dbg_" + name, list(shape), dt, kind="ExternalOutput")
            return dbg_h[name]
        return None

    x_res = nc.alloc_sbuf_tensor("x_res", [128, NCH * T], F32)
    xn_hi = nc.alloc_sbuf_tensor("xn_hi", [128, NCH * T], BF16)
    xn_lo = nc.alloc_sbuf_tensor("xn_lo", [128, NCH * T], BF16)
    af = nc.alloc_sbuf_tensor("af", [128, 5632], F32)
    ab = nc.alloc_sbuf_tensor("ab", [128, 18432], BF16)
    ai32 = nc.alloc_sbuf_tensor("ai32", [128, 512], mybir.dt.int32)
    consts = nc.alloc_sbuf_tensor("consts", [128, 132], F32)
    wsm = nc.alloc_sbuf_tensor("wsm", [128, 96], F32)
    r8 = nc.alloc_sbuf_tensor("r8", [8, 64], F32)
    wbuf = nc.alloc_sbuf_tensor("wbuf", [128, 4096], F32)
    fb1_sb = nc.alloc_sbuf_tensor("fb1_sb", [128, 32], F32)
    sb1_sb = nc.alloc_sbuf_tensor("sb1_sb", [128, 32], F32)
    eb1_sb = nc.alloc_sbuf_tensor("eb1_sb", [128, 32], F32)

    ps = [nc.alloc_psum_tensor("ps%d" % i, [128, 512], F32) for i in range(8)]

    kb = KB(nc)
    s_ld = kb.sem("s_ld")
    s_ld2 = kb.sem("s_ld2")
    s_gd = kb.sem("s_gd")
    s_t = kb.sem("s_t")
    s_v = kb.sem("s_v")
    s_s = kb.sem("s_s")
    s_g = kb.sem("s_g")
    s_cc = kb.sem("s_cc")
    ALL_SEMS = (s_ld, s_ld2, s_gd, s_t, s_v, s_s, s_g, s_cc)

    QSEM = {"sync": s_ld, "scalar": s_ld2, "gpsimd": s_gd}
    ESEM = {"vector": s_v, "scalar": s_s, "gpsimd": s_g}

    def dma(qeng, out_ap, in_ap, **kw):
        sem = QSEM[qeng]
        v = sem.inc(16)
        kb.on(qeng, lambda e, o=out_ap, i=in_ap, k=kw: e.dma_start(out=o, in_=i, **k).then_inc(sem.h, 16))
        return v

    def mm(out_ap, lhsT, rhs, start, stop, inc=False):
        if inc:
            v = s_t.inc()
            kb.on("tensor", lambda e, o=out_ap, l=lhsT, r=rhs, st=start, sp=stop:
                  e.matmul(o, l, r, start=st, stop=sp).then_inc(s_t.h, 1))
            return v
        kb.on("tensor", lambda e, o=out_ap, l=lhsT, r=rhs, st=start, sp=stop:
              e.matmul(o, l, r, start=st, stop=sp))
        return None

    def ev(eng, fn, inc=True):
        sem = ESEM[eng]
        if inc:
            v = sem.inc()
            kb.on(eng, lambda e, f=fn: f(e).then_inc(sem.h, 1))
            return v
        kb.on(eng, lambda e, f=fn: f(e))
        return None

    def fence():
        vals = [(s, s.n) for s in ALL_SEMS]
        for eng in ENGINES:
            own = ESEM.get(eng)
            for s, v in vals:
                if s is own:
                    continue
                kb.wait(eng, s, v)

    bank_read = {i: [] for i in range(8)}

    def bank_guard(i):
        for sem, val in bank_read[i]:
            kb.wait("tensor", sem, val)
        bank_read[i] = []

    def bank_reader(i, sem, val):
        bank_read[i].append((sem, val))

    CO = {}
    CO["ONES128"] = consts[:, 0:1]
    dma("sync", CO["ONES128"], dr["ONES128"].ap())
    CO["ONESROW"] = consts[0:1, 2:130]
    dma("sync", CO["ONESROW"], dr["ONESROW"].ap())
    CO["RMASK"] = r8[:, 32:40]
    dma("sync", CO["RMASK"], dr["RMASK"].ap())

    W = {}
    W["th"] = wsm[:, 0:4]; dma("sync", W["th"], dr["th"].ap())
    W["nu"] = wsm[:, 4:8]; dma("sync", W["nu"], dr["nu"].ap())
    W["ga"] = wsm[:, 8:12]; dma("sync", W["ga"], dr["ga"].ap())
    W["Dv"] = wsm[:, 12:20]; dma("sync", W["Dv"], dr["Dv"].ap())
    W["fb2"] = wsm[:, 20:28]; dma("sync", W["fb2"], dr["fb2"].ap())
    W["sb2"] = wsm[:, 28:36]; dma("sync", W["sb2"], dr["sb2"].ap())
    W["rb"] = wsm[0:8, 36:37]; dma("sync", W["rb"], dr["rb"].ap())
    dma("sync", fb1_sb[:, :], dr["fb1"].ap())
    dma("sync", sb1_sb[:, :], dr["sb1"].ap())
    W["eb2"] = wsm[:, 40:48]
    W["lr"] = wsm[:, 48:52]
    W["eg"] = wsm[:, 52:56]
    W["g1"] = wsm[:, 56:57]
    W["g2"] = wsm[:, 57:58]
    W["cre"] = wsm[:, 76:80]
    W["cim"] = wsm[:, 80:84]
    W["pfx1"] = wsm[:, 60:61]
    W["used"] = wsm[:, 61:62]
    W["pfx2"] = wsm[:, 62:63]
    W["ga1acc"] = wsm[:, 64:65]
    W["ga2acc"] = wsm[:, 65:66]
    W["gred"] = wsm[:, 66:67]
    W["grow"] = wsm[0:1, 67:69]

    dma("sync", sb3(x_res[:, :], NCH), fm3(dr["x_fm"].ap()))
    v_ld_init = s_ld.n
    W["eps"] = wsm[:, 69:70]
    ev("vector", lambda e: e.memset(W["eps"], EPSLN), inc=False)

    GSTRIP = (af[:, 4096:4160], af[:, 4160:4224])
    gam_last = {0: 0, 1: 0}
    for mi, wname in ((0, "W1T"), (1, "W2T")):
        flat = dr[wname].ap().rearrange("a b -> (a b)").rearrange("(n p q) -> n p q", p=128, q=512)
        strip = GSTRIP[mi]
        for ch in range(64):
            b = ch % 2
            buf = af[:, 4608 + b * 512: 5120 + b * 512]
            kb.wait("gpsimd", s_v, gam_last[b])
            vd = dma("gpsimd", buf, flat[ch])
            kb.wait("vector", s_gd, vd)
            gam_last[b] = ev("vector", lambda e, bu=buf, dc=strip[:, ch:ch + 1]: e.tensor_reduce(
                dc, bu, AXX, ALU.add, apply_absolute_value=True))
            dglog = dbg_out("glog", (128, 128))
            if dglog is not None:
                kb.wait("gpsimd", s_v, gam_last[b])
                vgl = dma("gpsimd", dglog.ap()[:, mi * 64 + ch: mi * 64 + ch + 1], strip[:, ch:ch + 1],
                          allow_slow_non_contiguous=True)
                kb.wait("vector", s_gd, vgl)
    v_gam_done = s_v.n

    def emit_layernorm():
        kb.wait("tensor", s_ld, v_ld_init)
        kb.wait("scalar", s_ld, v_ld_init)
        kb.wait("vector", s_ld, v_ld_init)
        sq = (af[:, 0:512], af[:, 512:1024])
        t1 = af[:, 1024:1536]
        v_sq = [0, 0]
        v_bc_prev = 0
        for tt in range(4):
            t0 = tt * 512
            bank_guard(0); bank_guard(1)
            pb_s = ps[0][0:1, 0:512]
            pb_q = ps[1][0:1, 0:512]
            for c in range(NCH):
                xsl = x_res[:, c * T + t0: c * T + t0 + 512]
                mm(pb_s, CO["ONES128"], xsl, start=(c == 0), stop=(c == NCH - 1))
                b = c % 2
                kb.wait("scalar", s_t, v_sq[b])
                vs = ev("scalar", lambda e, b_=b, xs=xsl: e.activation(sq[b_], xs, AF.Square))
                kb.wait("tensor", s_s, vs)
                v_sq[b] = mm(pb_q, CO["ONES128"], sq[b], start=(c == 0), stop=(c == NCH - 1), inc=True)
            v_sums = s_t.n
            mrow = af[0:1, 2048:2560]
            rrow = af[0:1, 2560:3072]
            u1 = af[0:1, 3072:3584]
            u2 = af[0:1, 3584:4096]
            kb.wait("vector", s_t, v_sums)
            kb.wait("vector", s_t, v_bc_prev)
            ev("vector", lambda e, m=mrow, p=pb_s: e.tensor_scalar(m, p, 1.0 / 1024.0, None, ALU.mult), inc=False)
            ev("vector", lambda e, q=u1, p=pb_q: e.tensor_scalar(q, p, 1.0 / 1024.0, None, ALU.mult), inc=False)
            ev("vector", lambda e, m=mrow, o=u2: e.tensor_tensor(o, m, m, ALU.mult), inc=False)
            v_var = ev("vector", lambda e, q=u1, o=u2: e.tensor_tensor(o, q, o, ALU.subtract))
            bank_reader(0, s_v, v_var); bank_reader(1, s_v, v_var)
            kb.wait("scalar", s_v, v_var)
            v_sd = ev("scalar", lambda e, i=u2, o=u1: e.activation(o, i, AF.Sqrt, bias=W["eps"][0:1, 0:1]))
            kb.wait("vector", s_s, v_sd)
            v_r = ev("vector", lambda e, i=u1, o=rrow: e.reciprocal(o, i))
            bank_guard(2); bank_guard(3)
            kb.wait("tensor", s_v, v_r)
            mm(ps[2][:, 0:512], CO["ONESROW"], mrow, start=True, stop=True)
            v_bc = mm(ps[3][:, 0:512], CO["ONESROW"], rrow, start=True, stop=True, inc=True)
            v_bc_prev = v_bc
            kb.wait("vector", s_t, v_bc)
            v_last = 0
            for c in range(NCH):
                xsl = x_res[:, c * T + t0: c * T + t0 + 512]
                hsl = xn_hi[:, c * T + t0: c * T + t0 + 512]
                lsl = xn_lo[:, c * T + t0: c * T + t0 + 512]
                ev("vector", lambda e, x=xsl: e.tensor_tensor(t1, x, ps[2][:, 0:512], ALU.subtract), inc=False)
                v_u = ev("vector", lambda e: e.tensor_tensor(t1, t1, ps[3][:, 0:512], ALU.mult))
                kb.wait("scalar", s_v, v_u)
                v_h = ev("scalar", lambda e, h=hsl: e.copy(h, t1))
                kb.wait("vector", s_s, v_h)
                v_last = ev("vector", lambda e, l=lsl, h=hsl: e.tensor_tensor(l, t1, h, ALU.subtract))
            bank_reader(2, s_v, v_last); bank_reader(3, s_v, v_last)

    emit_layernorm()
    d = dbg_out("xn0hi", (128, NCH * T), BF16)
    if d is not None:
        kb.wait("gpsimd", s_v, s_v.n)
        dma("gpsimd", d.ap(), xn_hi[:, :])
    fence()

    v0 = ev("scalar", lambda e: e.activation(W["lr"], W["nu"], AF.Exp))
    kb.wait("scalar", s_s, v0)
    ev("scalar", lambda e: e.activation(W["lr"], W["lr"], AF.Exp, scale=-1.0), inc=False)
    ev("scalar", lambda e: e.activation(W["eg"], W["ga"], AF.Exp), inc=False)
    v_eg = ev("scalar", lambda e: e.activation(W["th"], W["th"], AF.Exp))

    BuR = af[:, 0:512]
    BuI = af[:, 512:1024]
    CosP = af[:, 1024:1536]
    SinP = af[:, 1536:2048]
    S1 = af[:, 2048:2560]
    S2 = af[:, 2560:3072]
    IOT = af[:, 3072:3584]
    EVC = af[:, 3584:4096]

    def hplane(cc, which):
        base = (cc * 4 + which) * 512
        return ab[:, base:base + 512]

    WT = ab[:, 8192:12288]

    lru_spl_v = 0
    lru_spl_s = 0
    v_h_prev = 0
    for tq4 in range(4):
        tbase = tq4 * 512
        for cc in range(4):
            kb.wait("sync", s_v, lru_spl_v)
            kb.wait("sync", s_s, lru_spl_s)
            vw1 = dma("sync", sb3(wbuf[:, 0:1024], 8),
                      fm3(dr["BreT"].ap()[:, cc * 128:(cc + 1) * 128]))
            vw2 = dma("sync", sb3(wbuf[:, 1024:2048], 8),
                      fm3(dr["BimT"].ap()[:, cc * 128:(cc + 1) * 128]))
            kb.wait("scalar", s_ld, vw2)
            kb.wait("scalar", s_t, s_t.n)
            ev("scalar", lambda e: e.copy(WT[:, 0:1024], wbuf[:, 0:1024]), inc=False)
            vh2 = ev("scalar", lambda e: e.copy(WT[:, 1024:2048], wbuf[:, 1024:2048]))
            lru_spl_s = vh2
            kb.wait("vector", s_s, vh2)
            kb.wait("vector", s_ld, vw2)
            kb.wait("vector", s_t, s_t.n)
            ev("vector", lambda e: e.tensor_tensor(WT[:, 2048:3072], wbuf[:, 0:1024], WT[:, 0:1024], ALU.subtract), inc=False)
            vl2 = ev("vector", lambda e: e.tensor_tensor(WT[:, 3072:4096], wbuf[:, 1024:2048], WT[:, 1024:2048], ALU.subtract))
            lru_spl_v = vl2
            if tq4 == 0 and cc == 3:
                d_ = dbg_out("wt_rehi", (128, 1024), BF16)
                if d_ is not None:
                    vsync = ev("vector", lambda e: e.engine_nop())
                    kb.wait("gpsimd", s_v, vsync)
                    dma("gpsimd", d_.ap(), WT[:, 0:1024])
                    dma("gpsimd", dbg_out("wt_relo", (128, 1024), BF16).ap(), WT[:, 2048:3072])
                    vdbg = dma("gpsimd", dbg_out("wt_raw", (128, 1024)).ap(), wbuf[:, 0:1024])
                    kb.wait("vector", s_gd, vdbg)
                    kb.wait("scalar", s_gd, vdbg)
                    kb.wait("tensor", s_gd, vdbg)
            kb.wait("tensor", s_v, vl2)
            for ri in range(2):
                hi_off = ri * 1024
                lo_off = 2048 + ri * 1024
                bk = 6 + ri
                bank_guard(bk)
                for dc in range(NCH):
                    xh = xn_hi[:, dc * T + tbase: dc * T + tbase + 512]
                    xl = xn_lo[:, dc * T + tbase: dc * T + tbase + 512]
                    wh = WT[:, hi_off + dc * 128: hi_off + dc * 128 + 128]
                    wl = WT[:, lo_off + dc * 128: lo_off + dc * 128 + 128]
                    mm(ps[bk][:, 0:512], wh, xh, start=(dc == 0), stop=False)
                    mm(ps[bk][:, 0:512], wh, xl, start=False, stop=False)
                    vmm = mm(ps[bk][:, 0:512], wl, xh, start=False, stop=(dc == NCH - 1),
                             inc=(dc == NCH - 1))
                dst = BuR if ri == 0 else BuI
                kb.wait("vector", s_t, vmm)
                v_bu_ev = ev("vector", lambda e, d_=dst, b_=bk: e.tensor_copy(d_, ps[b_][:, 0:512]))
                bank_reader(bk, s_v, v_bu_ev)
            if tq4 == 0 and cc == 3:
                d_ = dbg_out("bu0re", (128, 512))
                if d_ is not None:
                    vsync = ev("vector", lambda e: e.engine_nop())
                    kb.wait("gpsimd", s_v, vsync)
                    dma("gpsimd", d_.ap(), BuR)
                    vdbg = dma("gpsimd", dbg_out("bu0im", (128, 512)).ap(), BuI)
                    kb.wait("vector", s_gd, vdbg)
            kb.wait("gpsimd", s_v, v_h_prev)
            v_io = ev("gpsimd", lambda e, b=tbase: e.iota(IOT, [[1, 512]], base=b, channel_multiplier=0,
                                                          allow_small_or_imprecise_dtypes=True))
            kb.wait("vector", s_g, v_io)
            thcol = W["th"][:, cc:cc + 1]
            egcol = W["eg"][:, cc:cc + 1]
            rcol = W["lr"][:, cc:cc + 1]
            kb.wait("vector", s_s, v_eg)
            ev("vector", lambda e, tc=thcol: e.tensor_scalar(S1, IOT, tc, None, ALU.mult), inc=False)
            ev("vector", lambda e: e.tensor_scalar(S2, S1, 1.0 / TWO_PI, None, ALU.mult), inc=False)
            ev("vector", lambda e: e.tensor_copy(ai32[:, 0:512], S2), inc=False)
            ev("vector", lambda e: e.tensor_copy(S2, ai32[:, 0:512]), inc=False)
            ev("vector", lambda e: e.scalar_tensor_tensor(S1, S2, -TWO_PI, S1, ALU.mult, ALU.add), inc=False)
            ev("vector", lambda e: e.tensor_scalar(S2, S1, PI, None, ALU.is_gt), inc=False)
            ev("vector", lambda e: e.scalar_tensor_tensor(S1, S2, -TWO_PI, S1, ALU.mult, ALU.add), inc=False)
            ev("vector", lambda e: e.tensor_scalar(S2, S1, -PI, None, ALU.is_lt), inc=False)
            v_a = ev("vector", lambda e: e.scalar_tensor_tensor(S1, S2, TWO_PI, S1, ALU.mult, ALU.add))
            kb.wait("scalar", s_v, v_a)
            ev("scalar", lambda e: e.activation(SinP, S1, AF.Sin), inc=False)
            v_s2 = ev("scalar", lambda e: e.activation(S2, S1, AF.Sin, scale=0.5))
            kb.wait("scalar", s_s, v_s2)
            v_sq2 = ev("scalar", lambda e: e.activation(S2, S2, AF.Square))
            kb.wait("vector", s_s, v_sq2)
            ev("vector", lambda e: e.tensor_scalar(CosP, S2, -2.0, 1.0, ALU.mult, ALU.add), inc=False)
            ev("vector", lambda e, g=egcol: e.tensor_scalar(BuR, BuR, g, None, ALU.mult), inc=False)
            ev("vector", lambda e, g=egcol: e.tensor_scalar(BuI, BuI, g, None, ALU.mult), inc=False)
            ev("vector", lambda e: e.tensor_tensor(S1, SinP, BuI, ALU.mult), inc=False)
            ev("vector", lambda e: e.tensor_tensor(S2, SinP, BuR, ALU.mult), inc=False)
            ev("vector", lambda e: e.tensor_tensor(BuR, CosP, BuR, ALU.mult), inc=False)
            ev("vector", lambda e: e.tensor_tensor(BuI, CosP, BuI, ALU.mult), inc=False)
            ev("vector", lambda e: e.tensor_tensor(BuR, BuR, S1, ALU.add), inc=False)
            ev("vector", lambda e: e.tensor_tensor(BuI, BuI, S2, ALU.subtract), inc=False)
            if tq4 == 0 and cc == 0:
                d_ = dbg_out("w0re", (128, 512))
                if d_ is not None:
                    vsync = ev("vector", lambda e: e.engine_nop())
                    kb.wait("gpsimd", s_v, vsync)
                    dma("gpsimd", d_.ap(), BuR)
                    vdbg = dma("gpsimd", dbg_out("w0im", (128, 512)).ap(), BuI)
                    kb.wait("vector", s_gd, vdbg)
            rbc = rcol.broadcast_to([128, 512])
            if tq4 == 0:
                ev("vector", lambda e, rb=rbc: e.tensor_tensor_scan(S1, rb, BuR, 0.0, ALU.mult, ALU.add), inc=False)
                ev("vector", lambda e, rb=rbc: e.tensor_tensor_scan(S2, rb, BuI, 0.0, ALU.mult, ALU.add), inc=False)
            else:
                ev("vector", lambda e, rb=rbc, c_=cc: e.tensor_tensor_scan(S1, rb, BuR, W["cre"][:, c_:c_ + 1], ALU.mult, ALU.add), inc=False)
                ev("vector", lambda e, rb=rbc, c_=cc: e.tensor_tensor_scan(S2, rb, BuI, W["cim"][:, c_:c_ + 1], ALU.mult, ALU.add), inc=False)
            if tq4 < 3:
                ev("vector", lambda e, c_=cc: e.tensor_copy(W["cre"][:, c_:c_ + 1], S1[:, 511:512]), inc=False)
                ev("vector", lambda e, c_=cc: e.tensor_copy(W["cim"][:, c_:c_ + 1], S2[:, 511:512]), inc=False)
            if tq4 == 0 and cc == 0:
                d_ = dbg_out("g0re", (128, 512))
                if d_ is not None:
                    vsync = ev("vector", lambda e: e.engine_nop())
                    kb.wait("gpsimd", s_v, vsync)
                    dma("gpsimd", d_.ap(), S1)
                    vdbg = dma("gpsimd", dbg_out("g0im", (128, 512)).ap(), S2)
                    kb.wait("vector", s_gd, vdbg)
            ev("vector", lambda e: e.tensor_tensor(BuR, CosP, S1, ALU.mult), inc=False)
            ev("vector", lambda e: e.tensor_tensor(BuI, CosP, S2, ALU.mult), inc=False)
            ev("vector", lambda e: e.tensor_tensor(IOT, SinP, S2, ALU.mult), inc=False)
            ev("vector", lambda e: e.tensor_tensor(BuR, BuR, IOT, ALU.subtract), inc=False)
            ev("vector", lambda e: e.tensor_tensor(IOT, SinP, S1, ALU.mult), inc=False)
            v_h = ev("vector", lambda e: e.tensor_tensor(BuI, BuI, IOT, ALU.add))
            v_h_prev = v_h
            kb.wait("scalar", s_v, v_h)
            ev("scalar", lambda e, o=hplane(cc, 0): e.copy(o, BuR), inc=False)
            vh_im = ev("scalar", lambda e, o=hplane(cc, 2): e.copy(o, BuI))
            kb.wait("vector", s_s, vh_im)
            ev("vector", lambda e, h=hplane(cc, 0), o=hplane(cc, 1):
               e.tensor_tensor(o, BuR, h, ALU.subtract), inc=False)
            v_hsplit = ev("vector", lambda e, h=hplane(cc, 2), o=hplane(cc, 3):
                          e.tensor_tensor(o, BuI, h, ALU.subtract))
            v_hsplit_s = vh_im
            if True:
                d_ = dbg_out("hq%d_%dre" % (tq4, cc), (128, 512))
                if d_ is not None:
                    vsync = ev("vector", lambda e: e.engine_nop())
                    kb.wait("gpsimd", s_v, vsync)
                    dma("gpsimd", d_.ap(), BuR)
                    vdbg = dma("gpsimd", dbg_out("hq%d_%dim" % (tq4, cc), (128, 512)).ap(), BuI)
                    kb.wait("vector", s_gd, vdbg)
        kb.wait("tensor", s_v, v_hsplit)
        kb.wait("tensor", s_s, v_hsplit_s)
        ctile_last = [s_t.n, s_t.n]
        c_spl_v = [lru_spl_v, lru_spl_v]
        c_spl_s = [lru_spl_s, lru_spl_s]
        for do in range(8):
            b = do % 2
            woff = b * 2048
            kb.wait("sync", s_t, ctile_last[b])
            kb.wait("sync", s_v, c_spl_v[b])
            kb.wait("sync", s_s, c_spl_s[b])
            dma("sync", sb3(wbuf[:, woff:woff + 512], 4),
                fm3(dr["CreT"].ap()[:, do * 128:(do + 1) * 128]))
            vw2 = dma("sync", sb3(wbuf[:, woff + 512:woff + 1024], 4),
                      fm3(dr["CimT"].ap()[:, do * 128:(do + 1) * 128]))
            kb.wait("scalar", s_ld, vw2)
            kb.wait("scalar", s_t, ctile_last[b])
            vh = ev("scalar", lambda e, w=woff: e.copy(WT[:, w:w + 512], wbuf[:, w:w + 512]))
            c_spl_s[b] = vh
            kb.wait("vector", s_s, vh)
            kb.wait("vector", s_ld, vw2)
            kb.wait("vector", s_t, ctile_last[b])
            ev("vector", lambda e, w=woff: e.tensor_tensor(
                WT[:, w + 1024:w + 1536], wbuf[:, w:w + 512], WT[:, w:w + 512], ALU.subtract), inc=False)
            ev("vector", lambda e, w=woff: e.tensor_scalar(
                WT[:, w + 512:w + 1024], wbuf[:, w + 512:w + 1024], -1.0, None, ALU.mult), inc=False)
            v_ct = ev("vector", lambda e, w=woff: e.scalar_tensor_tensor(
                WT[:, w + 1536:w + 2048], wbuf[:, w + 512:w + 1024], -1.0, WT[:, w + 512:w + 1024],
                ALU.mult, ALU.subtract))
            c_spl_v[b] = v_ct
            kb.wait("tensor", s_v, v_ct)
            bk = 4 + b
            bank_guard(bk)
            nmm = 0
            for cc in range(4):
                for ri in range(2):
                    hh = hplane(cc, 0 + 2 * ri)
                    hl = hplane(cc, 1 + 2 * ri)
                    wh = WT[:, woff + ri * 512 + cc * 128: woff + ri * 512 + cc * 128 + 128]
                    wl = WT[:, woff + 1024 + ri * 512 + cc * 128: woff + 1024 + ri * 512 + cc * 128 + 128]
                    mm(ps[bk][:, 0:512], wh, hh, start=(nmm == 0), stop=False)
                    mm(ps[bk][:, 0:512], wh, hl, start=False, stop=False)
                    last = (cc == 3 and ri == 1)
                    vmm = mm(ps[bk][:, 0:512], wl, hh, start=False, stop=last, inc=last)
                    nmm += 3
            ctile_last[b] = vmm
            dcol = W["Dv"][:, do:do + 1]
            xh = xn_hi[:, do * T + tbase: do * T + tbase + 512]
            xl = xn_lo[:, do * T + tbase: do * T + tbase + 512]
            xr = x_res[:, do * T + tbase: do * T + tbase + 512]
            kb.wait("vector", s_t, vmm)
            ev("vector", lambda e, a=xh, d_=dcol, b_=bk: e.scalar_tensor_tensor(
                EVC, a, d_, ps[b_][:, 0:512], ALU.mult, ALU.add), inc=False)
            ev("vector", lambda e, a=xl, d_=dcol: e.scalar_tensor_tensor(
                EVC, a, d_, EVC, ALU.mult, ALU.add), inc=False)
            v_acc = ev("vector", lambda e, x=xr: e.tensor_tensor(x, x, EVC, ALU.add))
            bank_reader(bk, s_v, v_acc)
        lru_spl_v = max(c_spl_v)
        lru_spl_s = max(c_spl_s)

    d = dbg_out("x_l0attn", (128, NCH * T))
    if d is not None:
        kb.wait("gpsimd", s_v, s_v.n)
        dma("gpsimd", d.ap(), x_res[:, :])
    fence()

    bank_guard(0); bank_guard(1)
    kb.wait("tensor", s_v, v_gam_done)
    mm(ps[0][0:1, 0:64], CO["ONES128"], GSTRIP[0], start=True, stop=True)
    v_gt = mm(ps[1][0:1, 0:64], CO["ONES128"], GSTRIP[1], start=True, stop=True, inc=True)
    kb.wait("vector", s_t, v_gt)
    ev("vector", lambda e: e.tensor_reduce(W["grow"][0:1, 0:1], ps[0][0:1, 0:64], AXX, ALU.add), inc=False)
    v_gq = ev("vector", lambda e: e.tensor_reduce(W["grow"][0:1, 1:2], ps[1][0:1, 0:64], AXX, ALU.add))
    bank_reader(0, s_v, v_gq); bank_reader(1, s_v, v_gq)
    kb.wait("vector", s_v, v_gq)
    ev("vector", lambda e: e.tensor_scalar(W["grow"][0:1, 0:1], W["grow"][0:1, 0:1],
                                           1.0 / float(D * FF), EPSQ, ALU.mult, ALU.add), inc=False)
    v_gr = ev("vector", lambda e: e.tensor_scalar(W["grow"][0:1, 1:2], W["grow"][0:1, 1:2],
                                                  1.0 / float(D * FF), EPSQ, ALU.mult, ALU.add))
    bank_guard(2)
    kb.wait("tensor", s_v, v_gr)
    v_gb = mm(ps[2][:, 0:2], CO["ONESROW"], W["grow"], start=True, stop=True, inc=True)
    kb.wait("vector", s_t, v_gb)
    v_gc0 = ev("vector", lambda e: e.tensor_copy(wsm[:, 56:58], ps[2][:, 0:2]))
    bank_reader(2, s_v, v_gc0)
    kb.wait("vector", s_v, v_gc0)
    v_gc = ev("vector", lambda e: e.reciprocal(wsm[:, 56:58], wsm[:, 56:58]))

    qf = (af[:, 0:512], af[:, 512:1024])
    qg = (af[:, 1024:1536], af[:, 1536:2048])
    qo = (ab[:, 0:512], ab[:, 512:1024])
    q_last = [s_v.n, s_v.n]
    qd_last = [0, 0]
    kb.wait("vector", s_v, v_gc)
    for mi, (wname, qname, gcol) in enumerate((("W1T", "W1q_h", W["g1"]), ("W2T", "W2q_h", W["g2"]))):
        flat = dr[wname].ap().rearrange("a b -> (a b)").rearrange("(n p q) -> n p q", p=128, q=512)
        qflat = dr[qname].ap().rearrange("a b -> (a b)").rearrange("(n p q) -> n p q", p=128, q=512)
        for ch in range(64):
            b = ch % 2
            kb.wait("sync", s_v, q_last[b])
            vd = dma("sync", qf[b], flat[ch])
            kb.wait("vector", s_ld, vd)
            kb.wait("vector", s_gd, qd_last[b])
            ev("vector", lambda e, b_=b, g=gcol: e.tensor_scalar(qf[b_], qf[b_], g, None, ALU.mult), inc=False)
            ev("vector", lambda e, b_=b: e.tensor_scalar(qg[b_], qf[b_], 0.5, None, ALU.is_ge), inc=False)
            ev("vector", lambda e, b_=b: e.tensor_scalar(qf[b_], qf[b_], -0.5, None, ALU.is_le), inc=False)
            q_last[b] = ev("vector", lambda e, b_=b: e.tensor_tensor(qo[b_], qg[b_], qf[b_], ALU.subtract))
            kb.wait("gpsimd", s_v, q_last[b])
            qd_last[b] = dma("gpsimd", qflat[ch], qo[b])
    fence()

    def hb(fci, which):
        base = which * 2048 + fci * SLAB
        return ab[:, base:base + SLAB]

    hscr = af[:, 2048:2048 + SLAB]
    kb.wait("sync", s_gd, max(qd_last))
    kb.wait("scalar", s_gd, max(qd_last))
    NTIL = NSLAB * NFC
    w1_last = [0, 0, 0, 0]
    w2_last = [0, 0, 0, 0]
    vw1 = [0] * (NTIL + 1)
    vw2 = [0] * (NTIL + 1)

    def w1_src(ti):
        return fm3(dr["W1q_h"].ap()[:, (ti % NFC) * 128:(ti % NFC) * 128 + 128])

    def w2_src(ti):
        return dr["W2q_h"].ap()[(ti % NFC) * 128:(ti % NFC) * 128 + 128, :]

    def issue_w1(ti):
        bb = ti % 4
        kb.wait("sync", s_t, w1_last[bb])
        vw1[ti] = dma("sync", sb3(ab[:, 4096 + bb * 1024:4096 + bb * 1024 + 1024], 8), w1_src(ti))

    def issue_w2(ti):
        bb = ti % 4
        kb.wait("scalar", s_t, w2_last[bb])
        vw2[ti] = dma("scalar", ab[:, 8192 + bb * 1024: 8192 + bb * 1024 + 1024], w2_src(ti))

    for ti in range(4):
        issue_w1(ti)
        issue_w2(ti)
    v_dum1 = None
    v_dum2 = None
    v_hlo = 0
    v_ygrp = 0
    for sl in range(NSLAB):
        t0 = sl * SLAB
        for grp in range(4):
            for fci in range(8):
                fc = grp * 8 + fci
                ti = sl * NFC + fc
                b = ti % 4
                bk = fc % 2
                bank_guard(bk)
                if ti + 2 < NTIL:
                    kb.wait("tensor", s_ld, vw1[ti + 2])
                else:
                    if v_dum1 is None:
                        v_dum1 = dma("sync", wbuf[0:1, 0:8], dr["G8"].ap()[0:1, 0:8])
                        v_dum1 = dma("sync", wbuf[0:1, 8:16], dr["G8"].ap()[0:1, 0:8])
                    kb.wait("tensor", s_ld, v_dum1)
                for dc in range(NCH):
                    xh = xn_hi[:, dc * T + t0: dc * T + t0 + SLAB]
                    xl = xn_lo[:, dc * T + t0: dc * T + t0 + SLAB]
                    wq_ = ab[:, 4096 + b * 1024 + dc * 128: 4096 + b * 1024 + dc * 128 + 128]
                    mm(ps[bk][:, 0:SLAB], wq_, xh, start=(dc == 0), stop=False)
                    vmm = mm(ps[bk][:, 0:SLAB], wq_, xl, start=False, stop=(dc == NCH - 1),
                             inc=(dc == NCH - 1))
                w1_last[b] = vmm
                if ti + 4 < NTIL:
                    issue_w1(ti + 4)
                kb.wait("scalar", s_t, vmm)
                kb.wait("scalar", s_t, v_ygrp)
                kb.wait("scalar", s_v, v_hlo)
                v_hrelu = ev("scalar", lambda e, f=fc, b_=bk: e.activation(hscr, ps[b_][:, 0:SLAB],
                                                                           AF.Relu, bias=fb1_sb[:, f:f + 1]))
                bank_reader(bk, s_s, v_hrelu)
                v_hh = ev("scalar", lambda e, f=fci: e.copy(hb(f, 0), hscr))
                kb.wait("vector", s_s, v_hh)
                kb.wait("vector", s_t, v_ygrp)
                v_hlo = ev("vector", lambda e, f=fci: e.tensor_tensor(hb(f, 1), hscr, hb(f, 0), ALU.subtract))
            kb.wait("tensor", s_v, v_hlo)
            if grp == 0:
                for do in range(8):
                    bank_guard(2 + do // 2)
            for fci in range(8):
                fc = grp * 8 + fci
                ti = sl * NFC + fc
                b = ti % 4
                if ti + 2 < NTIL:
                    kb.wait("tensor", s_ld2, vw2[ti + 2])
                else:
                    if v_dum2 is None:
                        v_dum2 = dma("scalar", wbuf[0:1, 16:24], dr["G8"].ap()[0:1, 0:8])
                        v_dum2 = dma("scalar", wbuf[0:1, 24:32], dr["G8"].ap()[0:1, 0:8])
                    kb.wait("tensor", s_ld2, v_dum2)
                for do in range(8):
                    bk = 2 + do // 2
                    off = (do % 2) * SLAB
                    wq_ = ab[:, 8192 + b * 1024 + do * 128: 8192 + b * 1024 + do * 128 + 128]
                    mm(ps[bk][:, off:off + SLAB], wq_, hb(fci, 0),
                       start=(fc == 0 and do % 2 == 0), stop=False)
                    vmm = mm(ps[bk][:, off:off + SLAB], wq_, hb(fci, 1), start=False,
                             stop=(fc == NFC - 1 and do % 2 == 1), inc=(do == 7))
                w2_last[b] = vmm
                if ti + 4 < NTIL:
                    issue_w2(ti + 4)
            v_ygrp = s_t.n
        v_ydone = s_t.n
        kb.wait("vector", s_t, v_ydone)
        for do in range(8):
            bk = 2 + do // 2
            off = (do % 2) * SLAB
            xr = x_res[:, do * T + t0: do * T + t0 + SLAB]
            v_acc = ev("vector", lambda e, x=xr, b_=bk, o=off, d_=do: e.scalar_tensor_tensor(
                x, ps[b_][:, o:o + SLAB], W["fb2"][:, d_:d_ + 1], x, ALU.add, ALU.add))
            bank_reader(bk, s_v, v_acc)

    d = dbg_out("w1q", (D, FF), BF16)
    if d is not None:
        kb.wait("gpsimd", s_gd, max(qd_last))
        dma("gpsimd", d.ap(), dr["W1q_h"].ap())
        dma("gpsimd", dbg_out("w2q", (FF, D), BF16).ap(), dr["W2q_h"].ap())
        dgc = dbg_out("gcols", (128, 2))
        if dgc is not None:
            dma("gpsimd", dgc.ap(), wsm[:, 56:58])

    d = dbg_out("x1", (128, NCH * T))
    if d is not None:
        kb.wait("gpsimd", s_v, s_v.n)
        dma("gpsimd", d.ap(), x_res[:, :])
    fence()

    emit_layernorm()
    v_ln1 = s_v.n
    kb.wait("sync", s_v, v_ln1)
    kb.wait("sync", s_s, s_s.n)
    v_spill = dma("sync", fm3(dr["x1_spill"].ap()), sb3(x_res[:, :], NCH))
    kb.wait("vector", s_ld, v_spill)
    v_zero = ev("vector", lambda e: e.memset(x_res[:, :], 0.0))
    d = dbg_out("xn1hi", (128, NCH * T), BF16)
    if d is not None:
        kb.wait("gpsimd", s_v, v_ln1)
        dma("gpsimd", d.ap(), xn_hi[:, :])
    fence()

    RO = {}
    ro_names = ["L", "RNK", "OHP", "OH2", "RS", "CUM", "PAR", "POSP", "TMP", "KEEP", "OVF",
                "OHS", "RS2", "CUM2", "POS2", "TMP2", "TAKE2", "A", "OVFF"]
    for j, n in enumerate(ro_names):
        RO[n] = af[:, j * 128:(j + 1) * 128]
    RNK2 = af[:, 2432:2560]
    CG8 = af[:, 2816:2944]
    CMLT = af[:, 2944:3072]
    CSUME = af[:, 3072:3200]
    CROT = af[:, 3200:3328]
    CB8 = af[0:8, 3328:3456]
    logits = af[0:8, 3456:5504]
    dma("sync", CG8, dr["G8"].ap())
    dma("sync", CMLT, dr["MLT"].ap())
    dma("sync", CSUME, dr["SUME"].ap())
    v_cm = dma("sync", CB8, dr["B8"].ap())
    v_sel = dma("sync", wbuf[0:8, 0:2048], dr["SEL"].ap())
    v_rw = dma("sync", sb3(wbuf[:, 2048:2112], 8), fm3(dr["rwT"].ap()))
    kb.wait("scalar", s_ld, v_rw)
    v_rwh = ev("scalar", lambda e: e.copy(ab[:, 0:64], wbuf[:, 2048:2112]))
    kb.wait("vector", s_s, v_rwh)
    kb.wait("vector", s_ld, v_rw)
    v_rwl = ev("vector", lambda e: e.tensor_tensor(ab[:, 64:128], wbuf[:, 2048:2112], ab[:, 0:64], ALU.subtract))
    kb.wait("tensor", s_v, v_rwl)
    for tt in range(4):
        t0 = tt * 512
        bank_guard(6)
        for dc in range(NCH):
            xh = xn_hi[:, dc * T + t0: dc * T + t0 + 512]
            xl = xn_lo[:, dc * T + t0: dc * T + t0 + 512]
            wh = ab[:, dc * 8: dc * 8 + 8]
            wl = ab[:, 64 + dc * 8: 64 + dc * 8 + 8]
            mm(ps[6][0:8, 0:512], wh, xh, start=(dc == 0), stop=False)
            mm(ps[6][0:8, 0:512], wh, xl, start=False, stop=False)
            v_lg = mm(ps[6][0:8, 0:512], wl, xh, start=False, stop=(dc == NCH - 1),
                      inc=(dc == NCH - 1))
        kb.wait("vector", s_t, v_lg)
        v_lgr = ev("vector", lambda e, t_=t0: e.tensor_scalar(
            logits[:, t_:t_ + 512], ps[6][0:8, 0:512], W["rb"], None, ALU.add))
        bank_reader(6, s_v, v_lgr)
    bank_guard(7)
    kb.wait("tensor", s_v, s_v.n)
    for i in range(16):
        v_pl = mm(ps[7][:, 0:128], wbuf[0:8, i * 128:(i + 1) * 128],
                  logits[:, i:T:16], start=(i == 0), stop=(i == 15), inc=(i == 15))
    kb.wait("vector", s_t, v_pl)
    v_L = ev("vector", lambda e: e.tensor_copy(RO["L"], ps[7][:, 0:128]))
    bank_reader(7, s_v, v_L)

    v_rot = 0
    for d_ in range(1, 8):
        bk = d_ % 2
        bank_guard(bk)
        kb.wait("sync", s_t, v_rot)
        v_rd = dma("sync", CROT, dr["ROT%d" % d_].ap())
        kb.wait("tensor", s_ld, v_rd)
        kb.wait("tensor", s_v, v_L if d_ == 1 else s_v.n)
        v_rot = mm(ps[bk][:, 0:128], CROT, RO["L"], start=True, stop=True, inc=True)
        kb.wait("vector", s_t, v_rot)
        if d_ == 1:
            v_rk = ev("vector", lambda e, b_=bk: e.tensor_tensor(RO["RNK"], ps[b_][:, 0:128], RO["L"], ALU.is_gt))
        else:
            rsrc, rdst = (RO["RNK"], RNK2) if d_ % 2 == 0 else (RNK2, RO["RNK"])
            ev("vector", lambda e, b_=bk: e.tensor_tensor(RO["TMP"], ps[b_][:, 0:128], RO["L"], ALU.is_gt), inc=False)
            v_rk = ev("vector", lambda e, s_=rsrc, d2=rdst: e.tensor_tensor(d2, s_, RO["TMP"], ALU.add))
        bank_reader(bk, s_v, v_rk)
    ev("vector", lambda e: e.tensor_scalar(RO["OHP"], RO["RNK"], 0.0, None, ALU.is_equal), inc=False)
    v_oh = ev("vector", lambda e: e.tensor_scalar(RO["OH2"], RO["RNK"], 1.0, None, ALU.is_equal))

    onesbc = CO["ONES128"].broadcast_to([128, 128])

    def cumsum_block(src, rs_dst, cum_dst, par_dst, out_dst, v_ready):
        bank_guard(0); bank_guard(1)
        kb.wait("tensor", s_v, v_ready)
        mm(ps[0][:, 0:128], CG8, src, start=True, stop=True)
        v2 = mm(ps[1][:, 0:128], CMLT, src, start=True, stop=True, inc=True)
        kb.wait("vector", s_t, v2)
        ev("vector", lambda e: e.tensor_copy(rs_dst, ps[0][:, 0:128]), inc=False)
        v4 = ev("vector", lambda e: e.tensor_copy(par_dst, ps[1][:, 0:128]))
        bank_reader(0, s_v, v4); bank_reader(1, s_v, v4)
        ev("vector", lambda e: e.tensor_tensor_scan(cum_dst, onesbc, rs_dst, 0.0, ALU.mult, ALU.add), inc=False)
        ev("vector", lambda e: e.tensor_tensor(cum_dst, cum_dst, rs_dst, ALU.subtract), inc=False)
        v5 = ev("vector", lambda e: e.tensor_tensor(out_dst, cum_dst, par_dst, ALU.add))
        return v5

    cumsum_block(RO["OHP"], RO["RS"], RO["CUM"], RO["PAR"], RO["POSP"], v_oh)
    v_tot1 = ev("vector", lambda e: e.tensor_reduce(RO["TMP"][:, 0:1], RO["RS"], AXX, ALU.add))
    kb.wait("gpsimd", s_v, v_tot1)
    v_h1 = dma("gpsimd", dr["ag_in"].ap(), RO["TMP"][0:128:16, 0:1])
    kb.wait("gpsimd", s_gd, v_h1)
    v_cc1 = s_cc.inc()
    kb.on("gpsimd", lambda e: e.collective_compute(
        "AllGather", ALU.bypass, replica_groups=[list(range(NCORES))],
        ins=[dr["ag_in"].ap().opt()], outs=[dr["ag_out"].ap().opt()]).then_inc(s_cc.h, 1))
    kb.wait("gpsimd", s_cc, v_cc1)
    v_agl = dma("gpsimd", r8[0:8, 0:8], dr["ag_out"].ap().rearrange("r q -> q r"),
                allow_slow_non_contiguous=True)
    kb.wait("vector", s_gd, v_agl)
    ev("vector", lambda e: e.tensor_tensor(r8[0:8, 8:16], r8[0:8, 0:8], CO["RMASK"], ALU.mult), inc=False)
    ev("vector", lambda e: e.tensor_reduce(r8[0:8, 16:17], r8[0:8, 8:16], AXX, ALU.add), inc=False)
    ev("vector", lambda e: e.tensor_reduce(r8[0:8, 17:18], r8[0:8, 0:8], AXX, ALU.add), inc=False)
    v_pfx = ev("vector", lambda e: e.tensor_scalar(r8[0:8, 17:18], r8[0:8, 17:18], 2048.0, None, ALU.min))
    bank_guard(0)
    kb.wait("tensor", s_v, v_pfx)
    v_b8 = mm(ps[0][:, 0:2], CB8, r8[0:8, 16:18], start=True, stop=True, inc=True)
    kb.wait("vector", s_t, v_b8)
    v_pu = ev("vector", lambda e: e.tensor_copy(wsm[:, 60:62], ps[0][:, 0:2]))
    bank_reader(0, s_v, v_pu)
    ev("vector", lambda e: e.tensor_scalar(RO["POSP"], RO["POSP"], W["pfx1"], None, ALU.add), inc=False)
    v_q = ev("vector", lambda e: e.tensor_tensor(RO["TMP"], RO["OHP"], RO["POSP"], ALU.mult))
    bank_guard(1)
    kb.wait("tensor", s_v, v_q)
    v_pt = mm(ps[1][:, 0:128], CSUME, RO["TMP"], start=True, stop=True, inc=True)
    kb.wait("vector", s_t, v_pt)
    ev("vector", lambda e: e.tensor_scalar(RO["KEEP"], ps[1][:, 0:128], 2048.0, None, ALU.is_lt), inc=False)
    v_ko = ev("vector", lambda e: e.tensor_scalar(RO["OVF"], ps[1][:, 0:128], 2048.0, None, ALU.is_ge))
    bank_reader(1, s_v, v_ko)
    v_ohs = ev("vector", lambda e: e.tensor_tensor(RO["OHS"], RO["OH2"], RO["OVF"], ALU.mult))
    cumsum_block(RO["OHS"], RO["RS2"], RO["CUM2"], RO["TMP2"], RO["POS2"], v_ohs)
    v_tot2 = ev("vector", lambda e: e.tensor_reduce(RO["TMP2"][:, 0:1], RO["RS2"], AXX, ALU.add))
    kb.wait("gpsimd", s_v, v_tot2)
    v_h2 = dma("gpsimd", dr["ag2_in"].ap(), RO["TMP2"][0:128:16, 0:1])
    kb.wait("gpsimd", s_gd, v_h2)
    v_cc2 = s_cc.inc()
    kb.on("gpsimd", lambda e: e.collective_compute(
        "AllGather", ALU.bypass, replica_groups=[list(range(NCORES))],
        ins=[dr["ag2_in"].ap().opt()], outs=[dr["ag2_out"].ap().opt()]).then_inc(s_cc.h, 1))
    kb.wait("gpsimd", s_cc, v_cc2)
    v_agl2 = dma("gpsimd", r8[0:8, 20:28], dr["ag2_out"].ap().rearrange("r q -> q r"),
                 allow_slow_non_contiguous=True)
    kb.wait("vector", s_gd, v_agl2)
    ev("vector", lambda e: e.tensor_tensor(r8[0:8, 28:36], r8[0:8, 20:28], CO["RMASK"], ALU.mult), inc=False)
    v_pf2 = ev("vector", lambda e: e.tensor_reduce(r8[0:8, 36:37], r8[0:8, 28:36], AXX, ALU.add))
    bank_guard(0)
    kb.wait("tensor", s_v, v_pf2)
    v_b82 = mm(ps[0][:, 0:1], CB8, r8[0:8, 36:37], start=True, stop=True, inc=True)
    kb.wait("vector", s_t, v_b82)
    v_p2 = ev("vector", lambda e: e.tensor_copy(wsm[:, 62:63], ps[0][:, 0:1]))
    bank_reader(0, s_v, v_p2)
    ev("vector", lambda e: e.tensor_scalar(RO["POS2"], RO["POS2"], W["pfx2"], None, ALU.add), inc=False)
    ev("vector", lambda e: e.tensor_scalar(RO["POS2"], RO["POS2"], W["used"], None, ALU.add), inc=False)
    v_r2 = ev("vector", lambda e: e.tensor_tensor(RO["TMP2"], RO["OHS"], RO["POS2"], ALU.mult))
    bank_guard(1)
    kb.wait("tensor", s_v, v_r2)
    v_r2t = mm(ps[1][:, 0:128], CSUME, RO["TMP2"], start=True, stop=True, inc=True)
    kb.wait("vector", s_t, v_r2t)
    ev("vector", lambda e: e.tensor_scalar(RO["TMP2"], ps[1][:, 0:128], 2048.0, None, ALU.is_lt), inc=False)
    v_t2 = ev("vector", lambda e: e.tensor_tensor(RO["TAKE2"], RO["OVF"], RO["TMP2"], ALU.mult))
    bank_reader(1, s_v, v_t2)
    ev("vector", lambda e: e.tensor_tensor(RO["TMP"], RO["OHP"], RO["KEEP"], ALU.mult), inc=False)
    ev("vector", lambda e: e.tensor_tensor(RO["TMP2"], RO["OHS"], RO["TAKE2"], ALU.mult), inc=False)
    ev("vector", lambda e: e.tensor_tensor(RO["A"], RO["TMP"], RO["TMP2"], ALU.add), inc=False)
    v_A = ev("vector", lambda e: e.tensor_tensor(RO["OVFF"], RO["OVF"], RO["TAKE2"], ALU.subtract))

    d = dbg_out("Aplane", (128, 128))
    if d is not None:
        kb.wait("gpsimd", s_v, v_A)
        dma("gpsimd", d.ap(), RO["A"])
        d2 = dbg_out("OVFF", (128, 128))
        dma("gpsimd", d2.ap(), RO["OVFF"])

    AMROW = af[0:9, 3456:5504]
    v_s2 = dma("sync", wbuf[:, 0:144], dr["SELA"].ap())
    v_s3 = dma("sync", wbuf[:, 144:288], dr["SELB"].ap())
    kb.wait("tensor", s_ld, v_s3)
    kb.wait("tensor", s_v, v_A)
    v_am = 0
    for i in range(16):
        bank_guard(6)
        mm(ps[6][0:9, 0:128], wbuf[:, i * 9:(i + 1) * 9], RO["A"], start=True, stop=False)
        v_am_t = mm(ps[6][0:9, 0:128], wbuf[:, 144 + i * 9:144 + (i + 1) * 9],
                    RO["OVFF"], start=False, stop=True, inc=True)
        kb.wait("vector", s_t, v_am_t)
        v_am = ev("vector", lambda e, i_=i: e.tensor_copy(AMROW[0:9, i_:T:16], ps[6][0:9, 0:128]))
        bank_reader(6, s_v, v_am)
    fence()

    Khi = ab[:, 0:2048]; Klo = ab[:, 2048:4096]
    VThi = ab[:, 4096:6144]; VTlo = ab[:, 6144:8192]
    Qhi = ab[:, 8192:8704]; Qlo = ab[:, 8704:9216]
    eshi = (ab[:, 9216:9728], ab[:, 9728:10240])
    eslo = (ab[:, 10240:10752], ab[:, 10752:11264])
    Ohi = ab[:, 11264:11776]; Olo = ab[:, 11776:12288]
    wt1 = ab[:, 12288:14336]
    wt2 = ab[:, 14336:16384]
    wup = ab[:, 16384:16896]
    ONESB = ab[:, 16896:16897]
    KWhi = ab[:, 9216:10240]; KWlo = ab[:, 10240:11264]
    VWhi = ab[:, 11264:12288]; VWlo = ab[:, 16897:17921]
    esf = (af[:, 0:512], af[:, 512:1024])
    Oevc = af[:, 1024:1536]
    Rbc = af[:, 1536:2048]

    v_ob = ev("vector", lambda e: e.memset(ONESB, 1.0))
    for p0 in (0, 32):
        dma("sync", wbuf[p0:p0 + 32, 1024:1152], dr["wku"].ap())
        v_vu = dma("sync", wbuf[p0:p0 + 32, 1152:1280], dr["wvu"].ap())
    kb.wait("scalar", s_ld, v_vu)
    ev("scalar", lambda e: e.copy(wup[0:64, 0:128], wbuf[0:64, 1024:1152]), inc=False)
    v_sh = ev("scalar", lambda e: e.copy(wup[0:64, 256:384], wbuf[0:64, 1152:1280]))
    kb.wait("vector", s_s, v_sh)
    kb.wait("vector", s_ld, v_vu)
    ev("vector", lambda e: e.tensor_tensor(wup[0:64, 128:256], wbuf[0:64, 1024:1152],
                                           wup[0:64, 0:128], ALU.subtract), inc=False)
    v_wl = ev("vector", lambda e: e.tensor_tensor(wup[0:64, 384:512], wbuf[0:64, 1152:1280],
                                                  wup[0:64, 256:384], ALU.subtract))

    v_wo_last = [0, 0]
    q_spl_v = 0
    q_spl_s = 0
    wo_spl_v = 0
    wo_spl_s = 0
    v_kwev = 0
    for kv in range(2):
        p0 = kv * 32
        kb.wait("sync", s_v, max(q_spl_v, wo_spl_v, v_wl))
        kb.wait("sync", s_s, max(q_spl_s, wo_spl_s))
        kb.wait("sync", s_t, s_t.n)
        v_kd = dma("sync", wbuf[0:64, 0:1024], dr["wkdT"].ap())
        v_vd = dma("sync", wbuf[0:64, 1024:2048], dr["wvdT"].ap())
        kb.wait("scalar", s_ld, v_vd)
        kb.wait("scalar", s_t, s_t.n)
        ev("scalar", lambda e: e.copy(wt1[0:64, 0:1024], wbuf[0:64, 0:1024]), inc=False)
        v_ds = ev("scalar", lambda e: e.copy(wt2[0:64, 0:1024], wbuf[0:64, 1024:2048]))
        kb.wait("vector", s_s, v_ds)
        kb.wait("vector", s_ld, v_vd)
        kb.wait("vector", s_t, s_t.n)
        ev("vector", lambda e: e.tensor_tensor(wt1[0:64, 1024:2048], wbuf[0:64, 0:1024],
                                               wt1[0:64, 0:1024], ALU.subtract), inc=False)
        v_dl = ev("vector", lambda e: e.tensor_tensor(wt2[0:64, 1024:2048], wbuf[0:64, 1024:2048],
                                                      wt2[0:64, 0:1024], ALU.subtract))
        q_spl_v = max(q_spl_v, v_dl)
        q_spl_s = max(q_spl_s, v_ds)
        wo_spl_v = max(wo_spl_v, v_dl)
        wo_spl_s = max(wo_spl_s, v_ds)
        kb.wait("tensor", s_v, v_dl)
        for li, (wdh, wdl, uo, dsth, dstl) in enumerate(
                ((wt1[:, 0:1024], wt1[:, 1024:2048], 0, KWhi, KWlo),
                 (wt2[:, 0:1024], wt2[:, 1024:2048], 256, VWhi, VWlo))):
            for dc in range(NCH):
                bk = dc % 2
                bank_guard(bk)
                lh = wdh[p0:p0 + 32, dc * 128:(dc + 1) * 128]
                ll = wdl[p0:p0 + 32, dc * 128:(dc + 1) * 128]
                uh = wup[p0:p0 + 32, uo:uo + 128]
                ul = wup[p0:p0 + 32, uo + 128:uo + 256]
                mm(ps[bk][:, 0:128], lh, uh, start=True, stop=False)
                mm(ps[bk][:, 0:128], lh, ul, start=False, stop=False)
                vmm = mm(ps[bk][:, 0:128], ll, uh, start=False, stop=True, inc=True)
                kb.wait("scalar", s_t, vmm)
                v_cwh = ev("scalar", lambda e, d_=dc, b_=bk, o=dsth: e.copy(
                    o[:, d_ * 128:(d_ + 1) * 128], ps[b_][:, 0:128]))
                kb.wait("vector", s_s, v_cwh)
                v_kwev = ev("vector", lambda e, d_=dc, b_=bk, oh=dsth, ol=dstl: e.tensor_tensor(
                    ol[:, d_ * 128:(d_ + 1) * 128], ps[b_][:, 0:128],
                    oh[:, d_ * 128:(d_ + 1) * 128], ALU.subtract))
                bank_reader(bk, s_v, v_kwev)
        kb.wait("tensor", s_v, v_kwev)
        for tt in range(4):
            t0 = tt * 512
            bk = tt % 2
            bank_guard(bk)
            for dc in range(NCH):
                xh = xn_hi[:, dc * T + t0: dc * T + t0 + 512]
                xl = xn_lo[:, dc * T + t0: dc * T + t0 + 512]
                wh = KWhi[:, dc * 128:(dc + 1) * 128]
                wl = KWlo[:, dc * 128:(dc + 1) * 128]
                mm(ps[bk][:, 0:512], wh, xh, start=(dc == 0), stop=False)
                mm(ps[bk][:, 0:512], wh, xl, start=False, stop=False)
                vmm = mm(ps[bk][:, 0:512], wl, xh, start=False, stop=(dc == NCH - 1),
                         inc=(dc == NCH - 1))
            kb.wait("scalar", s_t, vmm)
            v_kh = ev("scalar", lambda e, t_=t0, b_=bk: e.copy(Khi[:, t_:t_ + 512], ps[b_][:, 0:512]))
            kb.wait("vector", s_s, v_kh)
            v_kl = ev("vector", lambda e, t_=t0, b_=bk: e.tensor_tensor(
                Klo[:, t_:t_ + 512], ps[b_][:, 0:512], Khi[:, t_:t_ + 512], ALU.subtract))
            bank_reader(bk, s_v, v_kl)
        for tk in range(16):
            bk = tk % 2
            bank_guard(bk)
            for dc in range(NCH):
                xh = xn_hi[:, dc * T + tk * 128: dc * T + (tk + 1) * 128]
                xl = xn_lo[:, dc * T + tk * 128: dc * T + (tk + 1) * 128]
                wh = VWhi[:, dc * 128:(dc + 1) * 128]
                wl = VWlo[:, dc * 128:(dc + 1) * 128]
                mm(ps[bk][:, 0:128], xh, wh, start=(dc == 0), stop=False)
                mm(ps[bk][:, 0:128], xh, wl, start=False, stop=False)
                vmm = mm(ps[bk][:, 0:128], xl, wh, start=False, stop=(dc == NCH - 1),
                         inc=(dc == NCH - 1))
            kb.wait("scalar", s_t, vmm)
            v_vh = ev("scalar", lambda e, t_=tk, b_=bk: e.copy(VThi[:, t_ * 128:(t_ + 1) * 128], ps[b_][:, 0:128]))
            kb.wait("vector", s_s, v_vh)
            v_vl = ev("vector", lambda e, t_=tk, b_=bk: e.tensor_tensor(
                VTlo[:, t_ * 128:(t_ + 1) * 128], ps[b_][:, 0:128], VThi[:, t_ * 128:(t_ + 1) * 128], ALU.subtract))
            bank_reader(bk, s_v, v_vl)
        for rep in range(4):
            h = kv * 4 + rep
            kb.wait("sync", s_v, q_spl_v)
            kb.wait("sync", s_s, q_spl_s)
            kb.wait("sync", s_t, s_t.n)
            v_wq = dma("sync", sb3(wbuf[:, 0:1024], 8),
                       fm3(dr["wq"].ap()[:, h * 128:(h + 1) * 128]))
            kb.wait("scalar", s_ld, v_wq)
            kb.wait("scalar", s_t, s_t.n)
            q_spl_s = v_qh = ev("scalar", lambda e: e.copy(wt1[:, 0:1024], wbuf[:, 0:1024]))
            kb.wait("vector", s_s, v_qh)
            kb.wait("vector", s_ld, v_wq)
            kb.wait("vector", s_t, s_t.n)
            q_spl_v = v_ql = ev("vector", lambda e: e.tensor_tensor(wt1[:, 1024:2048], wbuf[:, 0:1024],
                                                                    wt1[:, 0:1024], ALU.subtract))
            kb.wait("scalar", s_v, wo_spl_v)
            kb.wait("scalar", s_t, max(v_wo_last))
            v_wo = dma("scalar", wbuf[:, 1024:2048], dr["wo"].ap()[h * 128:(h + 1) * 128, :])
            kb.wait("scalar", s_ld2, v_wo)
            wo_spl_s = v_woh = ev("scalar", lambda e: e.copy(wt2[:, 0:1024], wbuf[:, 1024:2048]))
            kb.wait("vector", s_s, v_woh)
            kb.wait("vector", s_ld2, v_wo)
            kb.wait("vector", s_t, max(v_wo_last))
            wo_spl_v = v_wol = ev("vector", lambda e: e.tensor_tensor(wt2[:, 1024:2048], wbuf[:, 1024:2048],
                                                                      wt2[:, 0:1024], ALU.subtract))
            for tq in range(4):
                q0 = tq * 512
                nkt = 4 * tq + 4
                kb.wait("tensor", s_v, v_ql)
                bkq = 5
                bank_guard(bkq)
                for dc in range(NCH):
                    xh = xn_hi[:, dc * T + q0: dc * T + q0 + 512]
                    xl = xn_lo[:, dc * T + q0: dc * T + q0 + 512]
                    wh = wt1[:, dc * 128:(dc + 1) * 128]
                    wl = wt1[:, 1024 + dc * 128: 1024 + dc * 128 + 128]
                    mm(ps[bkq][:, 0:512], wh, xh, start=(dc == 0), stop=False)
                    mm(ps[bkq][:, 0:512], wh, xl, start=False, stop=False)
                    vmm = mm(ps[bkq][:, 0:512], wl, xh, start=False, stop=(dc == NCH - 1),
                             inc=(dc == NCH - 1))
                kb.wait("scalar", s_t, vmm)
                v_qph = ev("scalar", lambda e: e.copy(Qhi, ps[bkq][:, 0:512]))
                kb.wait("vector", s_s, v_qph)
                v_qpl = ev("vector", lambda e: e.tensor_tensor(Qlo, ps[bkq][:, 0:512], Qhi, ALU.subtract))
                bank_reader(bkq, s_v, v_qpl)
                bank_guard(2)
                bank_guard(6)
                kb.wait("tensor", s_v, v_qpl)
                v_es_last = [0, 0]
                for kt in range(nkt):
                    bk = kt % 2
                    bank_guard(bk)
                    mm(ps[bk][:, 0:512], Khi[:, kt * 128:(kt + 1) * 128], Qhi, start=True, stop=False)
                    mm(ps[bk][:, 0:512], Khi[:, kt * 128:(kt + 1) * 128], Qlo, start=False, stop=False)
                    vmm = mm(ps[bk][:, 0:512], Klo[:, kt * 128:(kt + 1) * 128], Qhi,
                             start=False, stop=True, inc=True)
                    sb_ = kt % 2
                    kb.wait("scalar", s_t, vmm)
                    kb.wait("scalar", s_v, v_es_last[sb_])
                    v_exp = ev("scalar", lambda e, b_=bk, s2=sb_: e.activation(
                        esf[s2], ps[b_][:, 0:512], AF.Exp, scale=ATT_SCALE))
                    bank_reader(bk, s_s, v_exp)
                    if kt >= 4 * tq:
                        kb.wait("gpsimd", s_s, v_exp)
                        v_msk = ev("gpsimd", lambda e, s2=sb_, q_=q0, k_=kt: e.affine_select(
                            esf[s2], esf[s2], [[1, 512]], ALU.is_ge, 0.0,
                            base=q_ - k_ * 128, channel_multiplier=-1))
                        kb.wait("scalar", s_g, v_msk)
                        kb.wait("vector", s_g, v_msk)
                    v_eh = ev("scalar", lambda e, s2=sb_: e.copy(eshi[s2], esf[s2]))
                    kb.wait("vector", s_s, v_eh)
                    v_el = ev("vector", lambda e, s2=sb_: e.tensor_tensor(eslo[s2], esf[s2], eshi[s2], ALU.subtract))
                    v_es_last[sb_] = v_el
                    kb.wait("tensor", s_v, v_el)
                    mm(ps[6][0:1, 0:512], ONESB, eshi[sb_], start=(kt == 0), stop=False)
                    mm(ps[6][0:1, 0:512], ONESB, eslo[sb_], start=False, stop=(kt == nkt - 1))
                    mm(ps[2][:, 0:512], VThi[:, kt * 128:(kt + 1) * 128], eshi[sb_],
                       start=(kt == 0), stop=False)
                    mm(ps[2][:, 0:512], VThi[:, kt * 128:(kt + 1) * 128], eslo[sb_],
                       start=False, stop=False)
                    v_pv = mm(ps[2][:, 0:512], VTlo[:, kt * 128:(kt + 1) * 128], eshi[sb_],
                              start=False, stop=(kt == nkt - 1), inc=(kt == nkt - 1))
                kb.wait("vector", s_t, v_pv)
                ev("vector", lambda e: e.tensor_copy(af[0:1, 2048:2560], ps[6][0:1, 0:512]), inc=False)
                v_rc = ev("vector", lambda e: e.reciprocal(af[0:1, 2048:2560], af[0:1, 2048:2560]))
                bank_reader(6, s_v, v_rc)
                bank_guard(7)
                kb.wait("tensor", s_v, v_rc)
                v_bc = mm(ps[7][:, 0:512], CO["ONESROW"], af[0:1, 2048:2560], start=True, stop=True, inc=True)
                kb.wait("vector", s_t, v_bc)
                v_rb = ev("vector", lambda e: e.tensor_copy(Rbc, ps[7][:, 0:512]))
                bank_reader(7, s_v, v_rb)
                v_o = ev("vector", lambda e: e.tensor_tensor(Oevc, ps[2][:, 0:512], Rbc, ALU.mult))
                bank_reader(2, s_v, v_o)
                kb.wait("scalar", s_v, v_o)
                v_oh2 = ev("scalar", lambda e: e.copy(Ohi, Oevc))
                kb.wait("vector", s_s, v_oh2)
                v_ol = ev("vector", lambda e: e.tensor_tensor(Olo, Oevc, Ohi, ALU.subtract))
                kb.wait("tensor", s_v, max(v_ol, v_wol))
                for do in range(8):
                    bk = 3 + do % 2
                    bank_guard(bk)
                    mm(ps[bk][:, 0:512], wt2[:, do * 128:(do + 1) * 128], Ohi, start=True, stop=False)
                    mm(ps[bk][:, 0:512], wt2[:, do * 128:(do + 1) * 128], Olo, start=False, stop=False)
                    vmm = mm(ps[bk][:, 0:512], wt2[:, 1024 + do * 128: 1024 + do * 128 + 128], Ohi,
                             start=False, stop=True, inc=True)
                    v_wo_last[do % 2] = vmm
                    xr = x_res[:, do * T + q0: do * T + q0 + 512]
                    kb.wait("vector", s_t, vmm)
                    v_ac = ev("vector", lambda e, x=xr, b_=bk: e.tensor_tensor(x, x, ps[b_][:, 0:512], ALU.add))
                    bank_reader(bk, s_v, v_ac)
    fence()

    d = dbg_out("yattn", (128, NCH * T))
    if d is not None:
        kb.wait("gpsimd", s_v, s_v.n)
        dma("gpsimd", d.ap(), x_res[:, :])
        fence()

    mh = af[:, 0:SLAB]
    mh2 = af[:, SLAB:2 * SLAB]
    BSEL_SB = af[0:9, 1152:2304]
    B2ALL = af[0:9, 2304:3328]
    v_bsel = dma("sync", BSEL_SB, dr["BSEL"].ap())
    v_b2a = dma("sync", B2ALL, dr["B2R"].ap())

    def hbm_w1(e_):
        return dr["sw1"].ap() if e_ == 8 else dr["ew1"].ap()[e_]

    def hbm_w2(e_):
        return dr["sw2"].ap() if e_ == 8 else dr["ew2"].ap()[e_]

    w1_last = [0, 0]
    w2_last = [0, 0]
    w1_spl_s = [0, 0]
    w1_spl_v = [0, 0]
    w2_spl_s = [0, 0]
    w2_spl_v = [0, 0]
    v_mm2 = 0
    v_mhmask = 0
    v_eb1 = 0
    v_g = 0
    v_ygrp = 0
    for sl in range(NSLAB):
        t0 = sl * SLAB
        for do in range(8):
            bank_guard(2 + do // 2)
        for e_ in range(9):
            b1col = sb1_sb if e_ == 8 else eb1_sb
            if e_ < 8:
                kb.wait("sync", s_s, v_g)
                v_eb1 = dma("sync", eb1_sb[:, :], dr["eb1"].ap()[e_])
            bank_guard(6)
            kb.wait("tensor", s_ld, v_bsel)
            v_mask = mm(ps[6][:, 0:SLAB], BSEL_SB[:, e_ * 128:(e_ + 1) * 128],
                        AMROW[0:9, t0:t0 + SLAB], start=True, stop=True, inc=True)
            for grp in range(4):
                for fci in range(8):
                    fc = grp * 8 + fci
                    b = fc % 2
                    kb.wait("sync", s_s, w1_spl_s[b])
                    kb.wait("sync", s_v, w1_spl_v[b])
                    vw = dma("sync", sb3(wbuf[:, b * 1024:b * 1024 + 1024], 8),
                             fm3(hbm_w1(e_)[:, fc * 128:(fc + 1) * 128]))
                    kb.wait("scalar", s_ld, vw)
                    kb.wait("scalar", s_t, w1_last[b])
                    w1_spl_s[b] = v_sp1 = ev("scalar", lambda e, b_=b: e.copy(
                        ab[:, 4096 + b_ * 1024:4096 + b_ * 1024 + 1024],
                        wbuf[:, b_ * 1024:b_ * 1024 + 1024]))
                    kb.wait("vector", s_s, v_sp1)
                    kb.wait("vector", s_ld, vw)
                    kb.wait("vector", s_t, w1_last[b])
                    w1_spl_v[b] = v_sp2 = ev("vector", lambda e, b_=b: e.tensor_tensor(
                        ab[:, 6144 + b_ * 1024:6144 + b_ * 1024 + 1024],
                        wbuf[:, b_ * 1024:b_ * 1024 + 1024],
                        ab[:, 4096 + b_ * 1024:4096 + b_ * 1024 + 1024], ALU.subtract))
                    bk = fc % 2
                    bank_guard(bk)
                    kb.wait("tensor", s_v, v_sp2)
                    for dc in range(NCH):
                        xh = xn_hi[:, dc * T + t0: dc * T + t0 + SLAB]
                        xl = xn_lo[:, dc * T + t0: dc * T + t0 + SLAB]
                        wh = ab[:, 4096 + b * 1024 + dc * 128: 4096 + b * 1024 + dc * 128 + 128]
                        wl = ab[:, 6144 + b * 1024 + dc * 128: 6144 + b * 1024 + dc * 128 + 128]
                        mm(ps[bk][:, 0:SLAB], wh, xh, start=(dc == 0), stop=False)
                        mm(ps[bk][:, 0:SLAB], wh, xl, start=False, stop=False)
                        vmm = mm(ps[bk][:, 0:SLAB], wl, xh, start=False, stop=(dc == NCH - 1),
                                 inc=(dc == NCH - 1))
                    w1_last[b] = vmm
                    kb.wait("scalar", s_t, vmm)
                    kb.wait("scalar", s_t, v_ygrp)
                    kb.wait("scalar", s_ld, v_eb1)
                    kb.wait("scalar", s_v, v_mm2)
                    v_g = ev("scalar", lambda e, f=fc, b_=bk, bc=b1col: e.activation(
                        mh, ps[b_][:, 0:SLAB], AF.Gelu, bias=bc[:, f:f + 1]))
                    bank_reader(bk, s_s, v_g)
                    kb.wait("vector", s_s, v_g)
                    kb.wait("vector", s_t, v_mask)
                    v_mm2 = ev("vector", lambda e: e.tensor_tensor(mh2, mh, ps[6][:, 0:SLAB], ALU.mult))
                    kb.wait("scalar", s_v, v_mm2)
                    v_hh = ev("scalar", lambda e, f=fci: e.copy(hb(f, 0), mh2))
                    kb.wait("vector", s_s, v_hh)
                    kb.wait("vector", s_t, v_ygrp)
                    v_mhmask = ev("vector", lambda e, f=fci: e.tensor_tensor(hb(f, 1), mh2, hb(f, 0), ALU.subtract))
                kb.wait("tensor", s_v, v_mhmask)
                for fci in range(8):
                    fc = grp * 8 + fci
                    b = fc % 2
                    kb.wait("scalar", s_v, w2_spl_v[b])
                    vw = dma("scalar", wbuf[:, 2048 + b * 1024: 2048 + b * 1024 + 1024],
                             hbm_w2(e_)[fc * 128:(fc + 1) * 128, :])
                    kb.wait("scalar", s_ld2, vw)
                    kb.wait("scalar", s_t, w2_last[b])
                    w2_spl_s[b] = v_sp1 = ev("scalar", lambda e, b_=b: e.copy(
                        ab[:, 8192 + b_ * 1024:8192 + b_ * 1024 + 1024],
                        wbuf[:, 2048 + b_ * 1024:2048 + b_ * 1024 + 1024]))
                    kb.wait("vector", s_s, v_sp1)
                    kb.wait("vector", s_ld2, vw)
                    kb.wait("vector", s_t, w2_last[b])
                    w2_spl_v[b] = v_sp2 = ev("vector", lambda e, b_=b: e.tensor_tensor(
                        ab[:, 10240 + b_ * 1024:10240 + b_ * 1024 + 1024],
                        wbuf[:, 2048 + b_ * 1024:2048 + b_ * 1024 + 1024],
                        ab[:, 8192 + b_ * 1024:8192 + b_ * 1024 + 1024], ALU.subtract))
                    kb.wait("tensor", s_v, v_sp2)
                    for do in range(8):
                        bk = 2 + do // 2
                        off = (do % 2) * SLAB
                        wh = ab[:, 8192 + b * 1024 + do * 128: 8192 + b * 1024 + do * 128 + 128]
                        wl = ab[:, 10240 + b * 1024 + do * 128: 10240 + b * 1024 + do * 128 + 128]
                        mm(ps[bk][:, off:off + SLAB], wh, hb(fci, 0),
                           start=(e_ == 0 and fc == 0 and do % 2 == 0), stop=False)
                        mm(ps[bk][:, off:off + SLAB], wh, hb(fci, 1), start=False, stop=False)
                        vmm = mm(ps[bk][:, off:off + SLAB], wl, hb(fci, 0), start=False,
                                 stop=False, inc=(do == 7))
                    w2_last[b] = vmm
                v_ygrp = s_t.n
            bank_reader(6, s_v, v_mhmask)
        kb.wait("tensor", s_ld, v_b2a)
        v_bias = 0
        for do in range(8):
            bk = 2 + do // 2
            off = (do % 2) * SLAB
            v_bias = mm(ps[bk][:, off:off + SLAB], B2ALL[0:9, do * 128:(do + 1) * 128],
                        AMROW[0:9, t0:t0 + SLAB], start=False, stop=(do % 2 == 1), inc=True)
        kb.wait("vector", s_t, v_bias)
        for do in range(8):
            bk = 2 + do // 2
            off = (do % 2) * SLAB
            xr = x_res[:, do * T + t0: do * T + t0 + SLAB]
            v_ac = ev("vector", lambda e, x=xr, b_=bk, o=off: e.tensor_tensor(
                x, x, ps[b_][:, o:o + SLAB], ALU.add))
            bank_reader(bk, s_v, v_ac)
    fence()

    spl = (af[:, 0:512], af[:, 512:1024])
    outb = (af[:, 1024:1536], af[:, 1536:2048])
    spill_fm3 = fm3(dr["x1_spill"].ap())
    y_fm3 = fm3(dr["y_out"].ap())
    v_out = [0, 0]
    v_sum = [0, 0]
    for c in range(NCH):
        for tt in range(4):
            t0 = tt * 512
            b = (c * 4 + tt) % 2
            kb.wait("sync", s_v, v_sum[b])
            vd = dma("sync", spl[b], spill_fm3[:, c, t0:t0 + 512])
            kb.wait("vector", s_ld, vd)
            kb.wait("vector", s_ld, v_out[b])
            v_sum[b] = ev("vector", lambda e, b_=b, c_=c, t_=t0: e.tensor_tensor(
                outb[b_], x_res[:, c_ * T + t_: c_ * T + t_ + 512], spl[b_], ALU.add))
            kb.wait("sync", s_v, v_sum[b])
            v_out[b] = dma("sync", y_fm3[:, c, t0:t0 + 512], outb[b])
    kb.wait("sync", s_ld, s_ld.n)
    for eng in ("scalar", "vector", "tensor", "gpsimd"):
        kb.wait(eng, s_ld, s_ld.n)

    kb.finalize()
    return nc, dr, dbg_h


def kernel(**inputs):
    g = host_layouts(inputs)
    co = host_consts()
    nc, dr, _ = build_program()
    in_maps = []
    for c in range(NCORES):
        m = {}
        for k, v in g.items():
            if k == "_x_cores":
                m["x_fm"] = v[c]
            else:
                m[k] = v
        for k, v in co.items():
            m[k] = v
        m["RMASK"] = np.repeat((np.arange(8) < c).astype(np.float32)[None, :], 8, axis=0)
        in_maps.append(m)
    res = run_bass_kernel_spmd(nc, in_maps, core_ids=list(range(NCORES)))
    out = np.stack([np.ascontiguousarray(res.results[c]["y_out"].T) for c in range(NCORES)], axis=0)
    return out
